# version 08
# baseline (speedup 1.0000x reference)
import numpy as np
import ml_dtypes
import concourse.bass as bass
import concourse.mybir as mybir
from concourse.bass_utils import run_bass_kernel_spmd

F32 = mybir.dt.float32
BF16 = mybir.dt.bfloat16
AF = mybir.ActivationFunctionType
ALU = mybir.AluOpType
AX = mybir.AxisListType
BF = ml_dtypes.bfloat16

P = 128
HID = 2048
CC = HID // P
QH, KVH, HD, WINDOW = 16, 4, 128, 512
T_FULL, N_CORES = 8192, 8
OWN = T_FULL // N_CORES
HALO = WINDOW
LOC = OWN + HALO
OTC, LTC = OWN // P, LOC // P
F_MOE = 1408
FC = F_MOE // P
NEXP = 12
EPS = 1e-5
NEGV = np.float32(-1e30)

ENGINES = ("sync", "tensor", "vector", "scalar", "gpsimd")


class _Stop(Exception):
    pass



def _lhsT_stream(W):
    M, K = W.shape
    t = np.ascontiguousarray(W.T).reshape(K // P, P, M // P, P).transpose(2, 1, 0, 3)
    return np.ascontiguousarray(t).reshape(M // P, P, K)


def _rhs_cols_stream(W, cbw):
    M, K = W.shape
    t = np.ascontiguousarray(W.T).reshape(K // P, P, M // cbw, cbw).transpose(2, 0, 1, 3)
    return np.ascontiguousarray(t)


def _host_prep(inputs):
    x = np.asarray(inputs["L0_attn_x"], dtype=np.float32)[0]
    wq = np.asarray(inputs["L0_attn_wq"], dtype=np.float32)
    wk = np.asarray(inputs["L0_attn_wk"], dtype=np.float32)
    wv = np.asarray(inputs["L0_attn_wv"], dtype=np.float32)
    wo = np.asarray(inputs["L0_attn_wo"], dtype=np.float32)
    gate_w = np.asarray(inputs["L0_ffn_gate_w"], dtype=np.float32)
    efc1 = np.asarray(inputs["L0_ffn_expert_fc1"], dtype=np.float32)
    efc2 = np.asarray(inputs["L0_ffn_expert_fc2"], dtype=np.float32)
    eproj = np.asarray(inputs["L0_ffn_expert_proj"], dtype=np.float32)
    sfc1 = np.asarray(inputs["L0_ffn_shared_fc1"], dtype=np.float32)
    sfc2 = np.asarray(inputs["L0_ffn_shared_fc2"], dtype=np.float32)
    sproj = np.asarray(inputs["L0_ffn_shared_proj"], dtype=np.float32)
    sgw = np.asarray(inputs["L0_ffn_shared_gate_w"], dtype=np.float32)

    shared = {}
    shared["wq_s"] = _lhsT_stream(wq)
    shared["wk_s"] = _lhsT_stream(wk)
    shared["wv_s"] = np.ascontiguousarray(wv.T).reshape(CC, P, 512)
    shared["wo_s"] = _rhs_cols_stream(wo, 512)
    rtr9 = np.concatenate([gate_w, sgw], axis=0)
    r = np.ascontiguousarray(rtr9.T).reshape(CC, P, 9).transpose(1, 0, 2)
    shared["rtr9_s"] = np.ascontiguousarray(r).reshape(1, P, CC * 9)

    fc12 = np.empty((NEXP, 2 * FC, P, HID), dtype=np.float32)
    proj = np.empty((NEXP, 4, FC, P, 512), dtype=np.float32)
    for e in range(8):
        fc12[e, 0::2] = _lhsT_stream(efc1[e])
        fc12[e, 1::2] = _lhsT_stream(efc2[e])
        proj[e] = _rhs_cols_stream(eproj[e], 512)
    for c in range(4):
        sl = slice(F_MOE * c, F_MOE * (c + 1))
        fc12[8 + c, 0::2] = _lhsT_stream(np.ascontiguousarray(sfc1[sl]))
        fc12[8 + c, 1::2] = _lhsT_stream(np.ascontiguousarray(sfc2[sl]))
        proj[8 + c] = _rhs_cols_stream(np.ascontiguousarray(sproj[:, sl]), 512)
    shared["fc12_s"] = fc12
    shared["proj_s"] = proj

    shared["idn_bf"] = np.eye(P, dtype=np.float32).astype(BF)
    shared["idn_f32"] = np.eye(P, dtype=np.float32)
    oh9 = np.zeros((9, 9 * P), dtype=np.float32)
    for e in range(9):
        oh9[e, e * P:(e + 1) * P] = 1.0
    shared["oh9"] = oh9

    half = HD // 2
    inv = 1.0 / 10000.0 ** (np.arange(half, dtype=np.float32) / half)
    tt = np.arange(T_FULL, dtype=np.float32)
    fr = np.outer(tt, inv)
    cosT, sinT = np.cos(fr).astype(np.float32), np.sin(fr).astype(np.float32)
    scale = np.float32(1.0 / np.sqrt(HD))

    per_core = []
    for c in range(N_CORES):
        base = OWN * c - HALO
        xs = np.zeros((LOC, HID), dtype=np.float32)
        lo = max(base, 0)
        xs[lo - base:] = x[lo:base + LOC]
        idx = np.clip(np.arange(LOC) + base, 0, T_FULL - 1)
        kcos = np.ascontiguousarray(cosT[idx].T)
        ksin = np.ascontiguousarray(sinT[idx].T)
        qcos = np.ascontiguousarray(kcos[:, HALO:] * scale)
        qsin = np.ascontiguousarray(ksin[:, HALO:] * scale)
        m = np.full((P, OTC, 5, P), NEGV, dtype=np.float32)
        for qt in range(OTC):
            for cc5 in range(5):
                j0 = cc5 * P + np.arange(P)[:, None]
                q_ = np.arange(P)[None, :]
                r_ = qt * P + j0
                ok = (j0 >= q_) & (j0 <= q_ + WINDOW) & (r_ >= HALO - OWN * c)
                m[:, qt, cc5, :] = np.where(ok, 0.0, NEGV)
        per_core.append({
            "x": xs,
            "masks": np.ascontiguousarray(m.reshape(P, OTC * 5 * P)).astype(BF),
            "qcos": qcos, "qsin": qsin, "kcos": kcos, "ksin": ksin,
        })
    return shared, per_core



class Builder:
    def __init__(self, nc):
        self.nc = nc
        self.ops = {e: [] for e in ENGINES}
        self.cnt = {}
        self.S = {}
        self.war = {}

    def sem(self, name, stack):
        self.S[name] = stack.enter_context(self.nc.semaphore(name))
        self.cnt[name] = 0

    def inc(self, name, n=1):
        self.cnt[name] += n
        return self.cnt[name]

    def at(self, name):
        return self.cnt[name]


def _emit_waits(E, S, waits):
    seen = {}
    for s, v in waits:
        if v > 0 and seen.get(s, -1) < v:
            seen[s] = v
    for s, v in seen.items():
        E.wait_ge(S[s], v)


def build_program(nc, phases=9, taps=()):
    from contextlib import ExitStack
    stack = ExitStack()

    d = {}
    d["x"] = nc.dram_tensor("x", [LOC, HID], F32, kind="ExternalInput").ap()
    d["masks"] = nc.dram_tensor("masks", [P, OTC * 5 * P], BF16, kind="ExternalInput").ap()
    for n, w in [("qcos", OWN), ("qsin", OWN), ("kcos", LOC), ("ksin", LOC)]:
        d[n] = nc.dram_tensor(n, [64, w], F32, kind="ExternalInput").ap()
    d["idn_bf"] = nc.dram_tensor("idn_bf", [P, P], BF16, kind="ExternalInput").ap()
    d["idn_f32"] = nc.dram_tensor("idn_f32", [P, P], F32, kind="ExternalInput").ap()
    d["oh9"] = nc.dram_tensor("oh9", [9, 9 * P], F32, kind="ExternalInput").ap()
    d["wq_s"] = nc.dram_tensor("wq_s", [16, P, HID], F32, kind="ExternalInput").ap()
    d["wk_s"] = nc.dram_tensor("wk_s", [4, P, HID], F32, kind="ExternalInput").ap()
    d["wv_s"] = nc.dram_tensor("wv_s", [CC, P, 512], F32, kind="ExternalInput").ap()
    d["wo_s"] = nc.dram_tensor("wo_s", [4, 16, P, 512], F32, kind="ExternalInput").ap()
    d["rtr9_s"] = nc.dram_tensor("rtr9_s", [1, P, CC * 9], F32, kind="ExternalInput").ap()
    d["fc12_s"] = nc.dram_tensor("fc12_s", [NEXP, 2 * FC, P, HID], F32, kind="ExternalInput").ap()
    d["proj_s"] = nc.dram_tensor("proj_s", [NEXP, 4, FC, P, 512], F32, kind="ExternalInput").ap()
    d["out"] = nc.dram_tensor("out", [OWN, HID], F32, kind="ExternalOutput").ap()

    KB = 1024
    SBASE = 16512
    cur = [0]

    nc._salloc_map = {}

    def salloc(name, shape, dtype, at=None):
        if at is not None:
            cur[0] = at
        t = nc.alloc_sbuf_tensor_at(name, shape, dtype, offset=SBASE + cur[0])
        nbytes = int(np.prod(shape[1:])) * mybir.dt.size(dtype)
        nc._salloc_map[name] = (cur[0], nbytes)
        cur[0] += (nbytes + 31) // 32 * 32
        return t

    idn_bf = salloc("idn_bf_s", [P, P], BF16)
    idn_f32 = salloc("idn_f32_s", [P, P], F32)
    ones_b = salloc("ones_b", [P, 1], BF16)
    ones1 = salloc("ones1", [1, P], F32)
    wpair = [salloc(f"wpair{i}", [P, 4096], BF16) for i in range(3)]
    xn_tm0 = salloc("xn_tm", [P, HID], BF16)
    s1v = salloc("s1v", [P, 1], F32)
    s2v = salloc("s2v", [P, 1], F32)
    negm = salloc("negm", [P, 1], F32)
    varv = salloc("varv", [P, 1], F32)
    rstd = salloc("rstd", [P, 1], F32)
    nmr = salloc("nmr", [P, 1], F32)
    r_sb = [salloc(f"rsb{i}", [1, P], F32) for i in range(2)]
    rb_sb = [salloc(f"rbsb{i}", [P, P], F32) for i in range(2)]
    r9_tm = salloc("r9_tm", [P, OTC, 9], F32)
    cg_tm = salloc("cg_tm", [P, OTC, 9], F32)
    e_sb = salloc("e_sb", [P, 8], F32)
    pr_sb = salloc("pr_sb", [P, 8], F32)
    msk_sb = salloc("msk_sb", [P, 8], F32)
    ge_sb = salloc("ge_sb", [P, 8], F32)
    mx1 = salloc("mx1", [P, 1], F32)
    sden = salloc("sden", [P, 1], F32)
    rs9 = salloc("rs9", [P, 1], F32)
    m2v = salloc("m2v", [P, 1], F32)
    rtr9_sb = salloc("rtr9_sb", [P, CC * 9], BF16)
    s1v2 = salloc("s1v2", [P, 1], F32)
    s2v2 = salloc("s2v2", [P, 1], F32)
    negm2 = salloc("negm2", [P, 1], F32)
    varv2 = salloc("varv2", [P, 1], F32)
    rstd2 = salloc("rstd2", [P, 1], F32)
    nmr2 = salloc("nmr2", [P, 1], F32)
    O1 = (cur[0] + 31) // 32 * 32

    xn1 = salloc("xn1", [P, CC, LOC], BF16, at=O1)
    v_tm = salloc("v_tm", [P, LTC, 512], BF16)
    xn_tm2 = salloc("xn_tm2", [P, HID], BF16)
    x1 = salloc("x1", [P, OTC, HID], F32, at=O1)
    O2 = O1 + 65536

    qcos = salloc("qcos_s", [64, OWN], F32, at=O2)
    qsin = salloc("qsin_s", [64, OWN], F32)
    kcos = salloc("kcos_s", [64, LOC], F32)
    ksin = salloc("ksin_s", [64, LOC], F32)
    masks = salloc("masks_s", [P, OTC, 5, P], BF16)
    q_rope = salloc("q_rope", [P, QH, OWN], BF16)
    xn2 = salloc("xn2", [P, CC, OWN], BF16, at=O2)
    r9_fm = salloc("r9_fm", [9, OWN], F32)
    cg_fm = salloc("cg_fm", [9, OWN], F32)
    oh9_sb = salloc("oh9_sb", [9, 9 * P], F32)
    O3 = O2 + 63488

    attn_fm = salloc("attn_fm", [P, QH, OWN], BF16, at=O3)
    h_sb = salloc("h_sb", [P, FC, OWN], BF16, at=O3)
    cbc = [salloc(f"cbc{i}", [P, OWN], F32) for i in range(2)]
    O4 = O3 + 32768

    xtile = [salloc(f"xtile{i}", [P, HID], F32, at=O4 if i == 0 else None)
             for i in range(2)]
    assert cur[0] <= 212863, cur[0]
    k_rope = salloc("k_rope", [P, KVH, LOC], BF16, at=O4)
    rp = [salloc(f"rp{i}", [64, 512], F32, at=O3 if i == 0 else None)
          for i in range(2)]
    probsT = [salloc(f"probsT{i}", [P, 640], BF16, at=O4 + 12288 if i == 0 else None)
              for i in range(2)]
    assert cur[0] <= 212863, cur[0]
    silu_sb = [salloc(f"silu{i}", [P, 512], F32, at=O4 if i == 0 else None)
               for i in range(2)]
    t1_sb = [salloc(f"t1sb{i}", [P, 512], F32) for i in range(2)]
    assert cur[0] <= 212863, cur[0]

    pb = [stack.enter_context(nc.psum_tensor(f"pb{i}", [P, 512], F32)) for i in range(4)]
    pc = [stack.enter_context(nc.psum_tensor(f"pc{i}", [P, 512], F32)) for i in range(2)]
    pd = stack.enter_context(nc.psum_tensor("pd", [P, 512], F32))
    ptr = stack.enter_context(nc.psum_tensor("ptr", [P, 1024], BF16))

    B = Builder(nc)
    for s in ["xdma", "xdma0", "xdma1", "wdma0", "wdma1", "wdma2",
              "odma", "pe", "dve", "act"]:
        B.sem(s, stack)

    stopped = [False]

    def ckpt(k):
        if phases <= k:
            stopped[0] = True

    import os
    SERIAL = os.environ.get("K_SERIAL", "0")

    def task(eng, waits, body, incs=()):
        if stopped[0]:
            import collections
            return collections.defaultdict(int)
        w = list(waits)
        if SERIAL == "1" or SERIAL == eng:
            w = [(s, B.at(s)) for s in ("pe", "dve", "act")] + \
                [(s, B.at(s)) for s in ("xdma", "wdma", "odma")]
        ic = [(s, n, B.inc(s, n)) for s, n in incs]

        def run(E, w=w, ic=ic, body=body):
            _emit_waits(E, B.S, w)
            ins = body(E)
            for s, n, _ in ic:
                ins = ins.then_inc(B.S[s], n)
        B.ops[eng].append(run)
        return {s: v for s, n, v in ic}

    def rd(res):
        return B.war.get(res, [])

    def setrd(res, tok):
        B.war[res] = [(s, v) for s, v in tok.items()]

    tap_reg = {"xn1": xn1, "krope": k_rope, "qrope": q_rope, "vtm": v_tm,
               "attn": attn_fm, "x1": x1, "xn2": xn2, "cgtm": cg_tm, "cgfm": cg_fm,
               "rtm": r9_tm, "h": h_sb, "probs0": probsT[0], "probs1": probsT[1]}

    task("vector", [], lambda E: E.memset(ones_b[:], 1.0))
    task("vector", [], lambda E: E.memset(ones1[:], 1.0), incs=[("dve", 1)])
    for name, dst in [("idn_bf", idn_bf), ("idn_f32", idn_f32), ("masks", masks),
                      ("qcos", qcos), ("qsin", qsin), ("kcos", kcos), ("ksin", ksin)]:
        task("sync", [("xdma", B.at("xdma"))],
             lambda E, dst=dst, name=name: E.dma_start(out=dst[:], in_=d[name][:]),
             incs=[("xdma", 16)])
    T_const = B.at("xdma")

    ring_g = [0]

    def grouped(dram3, s0, nslab, F):
        g = ring_g[0]
        ring_g[0] += 1
        buf = g % 3
        sem = f"wdma{buf}"
        waits = rd(("wpair", buf))
        if nslab == 1:
            src = dram3[s0]
            dst = wpair[buf][:, 0:F]
        else:
            src = dram3[s0:s0 + nslab].rearrange("s p f -> p s f")
            dst = wpair[buf][:, 0:nslab * F].rearrange("p (s f) -> p s f", f=F)
        task("gpsimd", waits,
             lambda E, src=src, dst=dst: E.dma_start(out=dst, in_=src),
             incs=[(sem, 16)])
        return buf, (sem, B.at(sem))

    stats_sets = [(s1v, s2v, negm, varv, rstd, nmr), (s1v2, s2v2, negm2, varv2, rstd2, nmr2)]

    def ln_tile(src_ap, wait_src, dst_fm, tcol, src_tok=None, alt=0, xbuf=None):
        s1v, s2v, negm, varv, rstd, nmr = stats_sets[alt]
        xn_tm = xbuf if xbuf is not None else xn_tm0
        t0 = task("vector", wait_src,
                  lambda E, s1v=s1v: E.reduce_sum(s1v[:], src_ap, axis=AX.X),
                  incs=[("dve", 1)])
        xkey = ("xn_tm", xn_tm.name)
        ta = task("scalar", wait_src + rd(xkey),
                  lambda E, xn_tm=xn_tm, s2v=s2v: E.activation(xn_tm[:], src_ap, AF.Square,
                                                               accum_out=s2v[:]),
                  incs=[("act", 1)])
        t1 = task("vector", [("dve", t0["dve"])],
                  lambda E, negm=negm, s1v=s1v: E.tensor_scalar(negm[:], s1v[:],
                                                               float(-1.0 / HID), None, ALU.mult),
                  incs=[("dve", 1)])
        t2 = task("vector", [("dve", t1["dve"])],
                  lambda E, varv=varv, negm=negm: E.tensor_tensor(varv[:], negm[:], negm[:],
                                                                 ALU.mult),
                  incs=[("dve", 1)])
        t3 = task("vector", [("dve", t2["dve"]), ("act", ta["act"])],
                  lambda E, varv=varv, s2v=s2v: E.scalar_tensor_tensor(
                      varv[:], s2v[:], float(1.0 / HID), varv[:], ALU.mult, ALU.subtract),
                  incs=[("dve", 1)])
        tv = task("vector", [("dve", t3["dve"])],
                  lambda E, varv=varv: E.tensor_scalar(varv[:], varv[:], float(EPS), None,
                                                       ALU.add),
                  incs=[("dve", 1)])
        tl = task("scalar", [("dve", tv["dve"])],
                  lambda E, varv=varv: E.activation(varv[:], varv[:], AF.Ln),
                  incs=[("act", 1)])
        tv2 = task("vector", [("act", tl["act"])],
                   lambda E, varv=varv: E.tensor_scalar(varv[:], varv[:], -0.5, None, ALU.mult),
                   incs=[("dve", 1)])
        ta2 = task("scalar", [("dve", tv2["dve"])],
                   lambda E, rstd=rstd, varv=varv: E.activation(rstd[:], varv[:], AF.Exp),
                   incs=[("act", 1)])
        t4 = task("vector", [("act", ta2["act"]), ("dve", t1["dve"])],
                  lambda E, nmr=nmr, negm=negm, rstd=rstd: E.tensor_tensor(nmr[:], negm[:],
                                                                           rstd[:], ALU.mult),
                  incs=[("dve", 1)])
        tn = task("vector", [("dve", t4["dve"])] + rd(xkey),
                  lambda E, xn_tm=xn_tm, nmr=nmr, rstd=rstd: E.tensor_scalar(
                      xn_tm[:], src_ap, nmr[:, 0:1], rstd[:, 0:1], ALU.add, ALU.mult),
                  incs=[("dve", 1)])
        ev = None
        tp = None
        for gph in range(2):
            def tbody(E, gph=gph, xn_tm=xn_tm):
                for j in range(8):
                    cc = gph * 8 + j
                    E.transpose(ptr[:, j * P:(j + 1) * P], xn_tm[:, cc * P:(cc + 1) * P],
                                idn_bf[:])
                return E.matmul(pd[0:1, 0:1], idn_bf[0:1, 0:1], idn_bf[0:1, 0:1],
                                start=True, stop=True)
            tp = task("tensor", [("dve", tn["dve"]), ("xdma", T_const)] + rd(("ptr",)), tbody,
                      incs=[("pe", 1)])
            ev = task("vector", [("pe", tp["pe"])],
                      lambda E, gph=gph, tcol=tcol: E.tensor_copy(
                          dst_fm[:, gph * 8:(gph + 1) * 8, tcol:tcol + P],
                          ptr[:].rearrange("p (c q) -> p c q", q=P)),
                      incs=[("dve", 1)])
            setrd(("ptr",), ev)
        setrd(xkey, tp)
        if src_tok is not None:
            src_tok.clear()
            src_tok.update({"dve": tn["dve"], "act": ta["act"]})
        return ev

    T_xn1 = {}

    def do_ln1_tile(t):
        xb = t % 2
        sem = f"xdma{xb}"
        xw = list(rd(("xtile", xb)))
        tok_dma = task("sync", xw,
                       lambda E, t=t, xb=xb: E.dma_start(out=xtile[xb][:],
                                                         in_=d["x"][t * P:(t + 1) * P, :]),
                       incs=[(sem, 16)])
        stok = {}
        ev = ln_tile(xtile[xb][:], [(sem, tok_dma[sem])], xn1, t * P, src_tok=stok,
                     alt=t % 2, xbuf=(xn_tm0 if t % 2 == 0 else xn_tm2))
        setrd(("xtile", xb), stok)
        T_xn1[t] = ev["dve"]

    punit = [0]

    def fm_proj_unit(slab_fn, rhs_fn, wait_extra):
        u = punit[0]
        punit[0] += 1
        bi = u % 4
        bank = pb[bi]

        def body(E, bank=bank):
            ins = None
            for kc in range(CC):
                ins = E.matmul(bank[:], slab_fn(kc), rhs_fn(kc),
                               start=(kc == 0), stop=(kc == CC - 1))
            return ins
        tok = task("tensor", wait_extra + rd(("pb", bi)), body, incs=[("pe", 1)])
        return bank, bi, tok

    def rope_evac(bank, tok, cos_t, sin_t, th0, w, dst, head, c0):
        task("vector", [("pe", tok["pe"])],
             lambda E: E.tensor_tensor(rp[0][:, 0:w], bank[0:64, 0:w], cos_t[:, th0:th0 + w], ALU.mult))
        task("vector", [],
             lambda E: E.tensor_tensor(rp[1][:, 0:w], bank[64:128, 0:w], sin_t[:, th0:th0 + w], ALU.mult))
        task("vector", [],
             lambda E: E.tensor_tensor(dst[0:64, head, c0:c0 + w], rp[0][:, 0:w], rp[1][:, 0:w],
                                       ALU.subtract))
        task("vector", [],
             lambda E: E.tensor_tensor(rp[0][:, 0:w], bank[0:64, 0:w], sin_t[:, th0:th0 + w], ALU.mult))
        task("vector", [],
             lambda E: E.tensor_tensor(rp[1][:, 0:w], bank[64:128, 0:w], cos_t[:, th0:th0 + w], ALU.mult))
        return task("vector", [],
                    lambda E: E.tensor_tensor(dst[64:128, head, c0:c0 + w], rp[0][:, 0:w],
                                              rp[1][:, 0:w], ALU.add),
                    incs=[("dve", 1)])

    wk_groups = [grouped(d["wk_s"], gi * 2, 2, HID) for gi in range(2)]
    for t in range(LTC):
        do_ln1_tile(t)
    T_xn1done = B.at("dve")
    ckpt(1)
    klast = None
    kunits = [(mc, th) for th in range(3) for mc in range(4)
              if not (mc == 3 or (mc == 2 and th == 2))]
    kunits += [(3, 0), (3, 1), (2, 2), (3, 2)]
    for mc, th in kunits:
        buf, wtok = wk_groups[mc // 2]
        s = mc % 2
        late = (mc == 3 or (mc == 2 and th == 2))
        gate = T_xn1done if late else T_xn1[th * 4 + 3]
        bank, bi, tok = fm_proj_unit(
            lambda kc, buf=buf, s=s: wpair[buf][:, s * HID + kc * P:s * HID + (kc + 1) * P],
            lambda kc, th=th: xn1[:, kc, th * 512:(th + 1) * 512],
            [wtok, ("dve", gate)])
        klast = tok
        ev = rope_evac(bank, tok, kcos, ksin, th * 512, 512, k_rope, mc, th * 512)
        setrd(("pb", bi), ev)
    setrd(("wpair", wk_groups[0][0]), klast)
    setrd(("wpair", wk_groups[1][0]), klast)
    wv_groups = [grouped(d["wv_s"], gi * 8, 8, 512) for gi in range(2)]
    vtok = None
    for tc in range(LTC):
        u = punit[0]
        punit[0] += 1
        bi = u % 4
        bank = pb[bi]

        def vbody(E, tc=tc, bank=bank):
            ins = None
            for kc in range(CC):
                buf = wv_groups[kc // 8][0]
                ins = E.matmul(bank[:], xn1[:, kc, tc * P:(tc + 1) * P],
                               wpair[buf][:, (kc % 8) * 512:(kc % 8 + 1) * 512],
                               start=(kc == 0), stop=(kc == CC - 1))
            return ins
        wts = [wv_groups[0][1], wv_groups[1][1], ("dve", T_xn1[tc])]
        vtok = task("tensor", wts + rd(("pb", bi)), vbody, incs=[("pe", 1)])
        ev = task("vector", [("pe", vtok["pe"])],
                  lambda E, tc=tc, bank=bank: E.tensor_copy(v_tm[:, tc, :], bank[:]),
                  incs=[("dve", 1)])
        setrd(("pb", bi), ev)
    setrd(("wpair", wv_groups[0][0]), vtok)
    setrd(("wpair", wv_groups[1][0]), vtok)
    for gi in range(8):
        buf, wtok = grouped(d["wq_s"], gi * 2, 2, HID)
        last = None
        for s in range(2):
            mc = gi * 2 + s
            for th in range(2):
                bank, bi, tok = fm_proj_unit(
                    lambda kc, buf=buf, s=s: wpair[buf][:, s * HID + kc * P:s * HID + (kc + 1) * P],
                    lambda kc, th=th: xn1[:, kc, HALO + th * 512:HALO + (th + 1) * 512],
                    [wtok])
                last = tok
                ev = rope_evac(bank, tok, qcos, qsin, th * 512, 512, q_rope, mc, th * 512)
                setrd(("pb", bi), ev)
        setrd(("wpair", buf), last)
    T_p2done = B.at("dve")
    T_p2pe = B.at("pe")
    ckpt(2)

    pairs = [(h, qt) for h in range(QH) for qt in range(OTC)]
    T_sc, T_exp, T_dn, T_pv, T_rbc, T_r, T_rbev, T_attn = {}, {}, {}, {}, {}, {}, {}, {}

    def emit_sc(i):
        h, qt = pairs[i]
        g = h // (QH // KVH)
        sa, sb = pb[2 * (i % 2)], pb[2 * (i % 2) + 1]
        waits = [("dve", T_p2done), ("xdma", T_const)]
        if i - 2 >= 0:
            waits.append(("act", T_exp[i - 2]))

        def body(E, h=h, qt=qt, g=g, sa=sa, sb=sb):
            ins = None
            for cc5 in range(4):
                ins = E.matmul(sa[:, cc5 * P:(cc5 + 1) * P], idn_bf[:], masks[:, qt, cc5, :],
                               start=(cc5 == 0), stop=False)
            for cc5 in range(4):
                ins = E.matmul(sa[:, cc5 * P:(cc5 + 1) * P],
                               k_rope[:, g, (qt + cc5) * P:(qt + cc5 + 1) * P],
                               q_rope[:, h, qt * P:(qt + 1) * P],
                               start=False, stop=(cc5 == 3))
            ins = E.matmul(sb[:, 0:P], idn_bf[:], masks[:, qt, 4, :], start=True, stop=False)
            ins = E.matmul(sb[:, 0:P], k_rope[:, g, (qt + 4) * P:(qt + 5) * P],
                           q_rope[:, h, qt * P:(qt + 1) * P], start=False, stop=True)
            return ins
        T_sc[i] = task("tensor", waits, body, incs=[("pe", 1)])["pe"]

    def emit_exp(i):
        waits = [("pe", T_sc[i])]
        if i - 2 >= 0:
            waits.append(("pe", T_pv[i - 2]))
        sa, sb = pb[2 * (i % 2)], pb[2 * (i % 2) + 1]
        task("scalar", waits,
             lambda E, i=i, sa=sa: E.activation(probsT[i % 2][:, 0:512], sa[:], AF.Exp))
        T_exp[i] = task("scalar", [],
                        lambda E, i=i, sb=sb: E.activation(probsT[i % 2][:, 512:640], sb[:, 0:P],
                                                           AF.Exp),
                        incs=[("act", 1)])["act"]

    def emit_dn(i):
        waits = [("act", T_exp[i])]
        if i - 1 >= 0:
            waits.append(("dve", T_rbev[i - 1]))

        def body(E, i=i):
            ins = None
            for cc5 in range(5):
                ins = E.matmul(pd[0:1, 0:P], ones_b[:], probsT[i % 2][:, cc5 * P:(cc5 + 1) * P],
                               start=(cc5 == 0), stop=(cc5 == 4))
            return ins
        T_dn[i] = task("tensor", waits, body, incs=[("pe", 1)])["pe"]

    def emit_recip(i):
        waits = [("pe", T_dn[i])]
        if i - 2 >= 0:
            waits.append(("pe", T_rbc[i - 2]))
        T_r[i] = task("vector", waits,
                      lambda E, i=i: E.reciprocal(r_sb[i % 2][:], pd[0:1, 0:P]),
                      incs=[("dve", 1)])["dve"]

    def emit_pv(i):
        h, qt = pairs[i]
        g = h // (QH // KVH)
        waits = [("act", T_exp[i])]
        if i - 2 >= 0:
            waits.append(("dve", T_attn[i - 2]))

        def body(E, i=i, g=g, qt=qt):
            ins = None
            for cc5 in range(5):
                ins = E.matmul(pc[i % 2][:, 0:P],
                               v_tm[:, qt + cc5, g * P:(g + 1) * P],
                               probsT[i % 2][:, cc5 * P:(cc5 + 1) * P],
                               start=(cc5 == 0), stop=(cc5 == 4))
            return ins
        T_pv[i] = task("tensor", waits, body, incs=[("pe", 1)])["pe"]

    def emit_rbc(i):
        T_rbc[i] = task("tensor", [("dve", T_r[i])],
                        lambda E, i=i: E.matmul(pd[:, P:2 * P], ones1[:], r_sb[i % 2][:],
                                                start=True, stop=True),
                        incs=[("pe", 1)])["pe"]

    def emit_attn_evac(i):
        h, qt = pairs[i]
        T_rbev[i] = task("vector", [("pe", T_rbc[i]), ("pe", T_pv[i])],
                         lambda E, i=i: E.tensor_copy(rb_sb[i % 2][:], pd[:, P:2 * P]),
                         incs=[("dve", 1)])["dve"]
        T_attn[i] = task("vector", [("dve", T_rbev[i])],
                         lambda E, i=i, h=h, qt=qt: E.tensor_tensor(
                             attn_fm[:, h, qt * P:(qt + 1) * P], pc[i % 2][:, 0:P],
                             rb_sb[i % 2][:], ALU.mult),
                         incs=[("dve", 1)])["dve"]

    emit_sc(0)
    emit_exp(0)
    emit_sc(1)
    for i in range(128):
        if i >= 1:
            emit_exp(i)
        emit_dn(i)
        emit_recip(i)
        emit_pv(i)
        emit_rbc(i)
        emit_attn_evac(i)
        if i + 2 < 128:
            emit_sc(i + 2)
    B.war[("pb", 0)] = [("act", T_exp[126])]
    B.war[("pb", 1)] = [("act", T_exp[126])]
    B.war[("pb", 2)] = [("act", T_exp[127])]
    B.war[("pb", 3)] = [("act", T_exp[127])]
    B.war[("pd",)] = [("dve", T_rbev[127])]
    B.war[("pc", 0)] = [("dve", T_attn[126])]
    B.war[("pc", 1)] = [("dve", T_attn[127])]
    T_attn_all = T_attn[127]
    T_pv_all = T_pv[127]
    ckpt(3)

    T_x1ld = {}
    for tc in range(OTC):
        gate = T_p2pe if tc <= 5 else T_pv_all
        T_x1ld[tc] = task("sync", [("pe", gate), ("xdma", B.at("xdma"))],
                          lambda E, tc=tc: E.dma_start(out=x1[:, tc, :],
                                                       in_=d["x"][(4 + tc) * P:(5 + tc) * P, :]),
                          incs=[("xdma", 16)])["xdma"]
    u4 = [0]
    T_x1 = {}
    for hb in range(4):
        gA = grouped(d["wo_s"][hb], 0, 8, 512)
        gB = grouped(d["wo_s"][hb], 8, 8, 512)
        last = None
        for tc in range(OTC):
            u = u4[0]
            u4[0] += 1
            bi = u % 4
            bank = pb[bi]
            waits = [("dve", T_attn_all), gA[1], gB[1]] + rd(("pb", bi))

            def body(E, tc=tc, bank=bank, gA=gA, gB=gB):
                ins = None
                for ac in range(CC):
                    buf = gA[0] if ac < 8 else gB[0]
                    ins = E.matmul(bank[:],
                                   attn_fm[:, ac, tc * P:(tc + 1) * P],
                                   wpair[buf][:, (ac % 8) * 512:(ac % 8 + 1) * 512],
                                   start=(ac == 0), stop=(ac == CC - 1))
                return ins
            tok = task("tensor", waits, body, incs=[("pe", 1)])
            last = tok
            ev = task("vector", [("pe", tok["pe"]), ("xdma", T_x1ld[tc])],
                      lambda E, tc=tc, hb=hb, bank=bank: E.tensor_tensor(
                          x1[:, tc, hb * 512:(hb + 1) * 512], bank[:],
                          x1[:, tc, hb * 512:(hb + 1) * 512], ALU.add),
                      incs=[("dve", 1)])
            setrd(("pb", bi), ev)
            if hb == 3:
                T_x1[tc] = ev["dve"]
        setrd(("wpair", gA[0]), last)
        setrd(("wpair", gB[0]), last)

    ckpt(4)

    for tc in range(OTC):
        ev = ln_tile(x1[:, tc, :], [("dve", T_x1[tc])], xn2, tc * P)
    T_xn2done = B.at("dve")

    oh_tok = task("sync", [("pe", T_sc[127]), ("xdma", B.at("xdma"))],
                  lambda E: E.dma_start(out=oh9_sb[:], in_=d["oh9"][:]), incs=[("xdma", 16)])
    rbuf, rtok = grouped(d["rtr9_s"], 0, 1, CC * 9)
    trc = task("vector", [rtok],
               lambda E: E.tensor_copy(rtr9_sb[:], wpair[rbuf][:, 0:CC * 9]),
               incs=[("dve", 1)])
    B.war[("wpair", rbuf)] = [("dve", trc["dve"])]
    T_rtrcp = trc["dve"]

    for th in range(2):
        bank = pb[th]

        def body(E, th=th, bank=bank):
            ins = None
            for kc in range(CC):
                ins = E.matmul(bank[0:9, :], rtr9_sb[:, kc * 9:(kc + 1) * 9],
                               xn2[:, kc, th * 512:(th + 1) * 512],
                               start=(kc == 0), stop=(kc == CC - 1))
            return ins
        tok = task("tensor", [("dve", max(T_xn2done, T_rtrcp))] + rd(("pb", th)), body,
                   incs=[("pe", 1)])
        ev = task("vector", [("pe", tok["pe"])],
                  lambda E, th=th, bank=bank: E.tensor_copy(r9_fm[:, th * 512:(th + 1) * 512],
                                                            bank[0:9, :]),
                  incs=[("dve", 1)])
        setrd(("pb", th), ev)
    T_r9 = B.at("dve")

    for tc in range(OTC):
        def rbody(E, tc=tc):
            E.transpose(pc[0][:, 0:9], r9_fm[:, tc * P:(tc + 1) * P], idn_f32[0:9, 0:9])
            return E.matmul(pd[0:1, 0:1], idn_bf[0:1, 0:1], idn_bf[0:1, 0:1],
                            start=True, stop=True)
        tp = task("tensor", [("dve", T_r9)] + rd(("pc", 0)), rbody, incs=[("pe", 1)])
        ev = task("vector", [("pe", tp["pe"])],
                  lambda E, tc=tc: E.tensor_copy(r9_tm[:, tc, :], pc[0][:, 0:9]),
                  incs=[("dve", 1)])
        setrd(("pc", 0), ev)
    T_rtm = B.at("dve")

    def step(eng, prev, body):
        sem = "dve" if eng == "vector" else "act"
        return task(eng, prev, body, incs=[(sem, 1)])

    for tc in range(OTC):
        t = step("vector", [("dve", T_rtm)],
                 lambda E, tc=tc: E.reduce_max(mx1[:], r9_tm[:, tc, 0:8], axis=AX.X, negate=True))
        t = step("scalar", [("dve", t["dve"])],
                 lambda E, tc=tc: E.activation(e_sb[:], r9_tm[:, tc, 0:8], AF.Exp, bias=mx1[:, 0:1]))
        t = step("vector", [("act", t["act"])],
                 lambda E: E.reduce_sum(sden[:], e_sb[:], axis=AX.X))
        t = step("vector", [("dve", t["dve"])], lambda E: E.reciprocal(rs9[:], sden[:]))
        t = step("vector", [("dve", t["dve"])],
                 lambda E: E.tensor_scalar(pr_sb[:], e_sb[:], rs9[:, 0:1], None, ALU.mult))
        t = step("vector", [("dve", t["dve"])],
                 lambda E: E.reduce_max(mx1[:], pr_sb[:], axis=AX.X))
        t = step("vector", [("dve", t["dve"])],
                 lambda E: E.tensor_scalar(msk_sb[:], pr_sb[:], mx1[:, 0:1], None, ALU.is_equal))
        t = step("vector", [("dve", t["dve"])],
                 lambda E: E.scalar_tensor_tensor(msk_sb[:], msk_sb[:], float(-1e30),
                                                  pr_sb[:], ALU.mult, ALU.add))
        t = step("vector", [("dve", t["dve"])],
                 lambda E: E.reduce_max(m2v[:], msk_sb[:], axis=AX.X))
        t = step("vector", [("dve", t["dve"])],
                 lambda E: E.tensor_scalar(ge_sb[:], pr_sb[:], m2v[:, 0:1], None, ALU.is_ge))
        t = step("vector", [("dve", t["dve"])],
                 lambda E, tc=tc: E.tensor_tensor(cg_tm[:, tc, 0:8], ge_sb[:], pr_sb[:], ALU.mult))
    T_comb = B.at("dve")
    for tc in range(OTC):
        task("scalar", [("dve", T_comb)],
             lambda E, tc=tc: E.activation(cg_tm[:, tc, 8:9], r9_tm[:, tc, 8:9], AF.Sigmoid),
             incs=[("act", 1)])
    T_sig = B.at("act")
    for tc in range(OTC):
        def cbody(E, tc=tc):
            E.transpose(pc[1][0:9, 0:P], cg_tm[:, tc, :], idn_f32[:])
            return E.matmul(pd[0:1, 0:1], idn_bf[0:1, 0:1], idn_bf[0:1, 0:1],
                            start=True, stop=True)
        tp = task("tensor", [("act", T_sig), ("dve", T_comb)] + rd(("pc", 1)), cbody,
                  incs=[("pe", 1)])
        ev = task("vector", [("pe", tp["pe"])],
                  lambda E, tc=tc: E.tensor_copy(cg_fm[:, tc * P:(tc + 1) * P], pc[1][0:9, 0:P]),
                  incs=[("dve", 1)])
        setrd(("pc", 1), ev)
    T_cg = B.at("dve")
    ckpt(5)

    u6 = [0]
    up = [0]
    T_out = {}
    for e in range(NEXP):
        row = e if e < 8 else 8
        T_cbc = None
        for th in range(2):
            tok = task("tensor", [("xdma", oh_tok["xdma"]), ("dve", T_cg)] + rd(("pd",)),
                       lambda E, row=row, th=th: E.matmul(
                           pd[:], oh9_sb[:, row * P:(row + 1) * P],
                           cg_fm[:, th * 512:(th + 1) * 512], start=True, stop=True),
                       incs=[("pe", 1)])
            ev = task("vector", [("pe", tok["pe"])],
                      lambda E, e=e, th=th: E.tensor_copy(cbc[e % 2][:, th * 512:(th + 1) * 512],
                                                          pd[:]),
                      incs=[("dve", 1)])
            setrd(("pd",), ev)
            T_cbc = ev["dve"]

        for f in range(FC):
            buf, wtok = grouped(d["fc12_s"][e], f * 2, 2, HID)
            lastok = None
            for th in range(2):
                u = u6[0]
                u6[0] += 1
                b1i, b2i = 2 * (u % 2), 2 * (u % 2) + 1
                b1, b2 = pb[b1i], pb[b2i]
                waits = [wtok, ("dve", T_xn2done)] + rd(("pb", b1i)) + rd(("pb", b2i))

                def body(E, buf=buf, th=th, b1=b1, b2=b2):
                    ins = None
                    for kc in range(CC):
                        E.matmul(b1[:], wpair[buf][:, kc * P:(kc + 1) * P],
                                 xn2[:, kc, th * 512:(th + 1) * 512],
                                 start=(kc == 0), stop=(kc == CC - 1))
                    for kc in range(CC):
                        ins = E.matmul(b2[:], wpair[buf][:, HID + kc * P:HID + (kc + 1) * P],
                                       xn2[:, kc, th * 512:(th + 1) * 512],
                                       start=(kc == 0), stop=(kc == CC - 1))
                    return ins
                tok = task("tensor", waits, body, incs=[("pe", 1)])
                lastok = tok
                ta = task("scalar", [("pe", tok["pe"])],
                          lambda E, u=u, b1=b1: E.activation(silu_sb[u % 2][:], b1[:], AF.Silu),
                          incs=[("act", 1)])
                task("vector", [("act", ta["act"]), ("dve", T_cbc)],
                     lambda E, u=u, b2=b2: E.tensor_tensor(t1_sb[u % 2][:], b2[:],
                                                           silu_sb[u % 2][:], ALU.mult))
                ev = task("vector", [],
                          lambda E, e=e, f=f, th=th, u=u: E.tensor_tensor(
                              h_sb[:, f, th * 512:(th + 1) * 512], t1_sb[u % 2][:],
                              cbc[e % 2][:, th * 512:(th + 1) * 512], ALU.mult),
                          incs=[("dve", 1)])
                setrd(("pb", b1i), ev)
                setrd(("pb", b2i), ev)
            setrd(("wpair", buf), lastok)
        T_h = B.at("dve")

        for cb in range(4):
            gA = grouped(d["proj_s"][e, cb], 0, 8, 512)
            gB = grouped(d["proj_s"][e, cb], 8, 3, 512)
            lastok = None
            for tc in range(OTC):
                u = up[0]
                up[0] += 1
                bi = u % 2
                bank = pc[bi]
                waits = [("dve", T_h), gA[1], gB[1]] + rd(("pc", bi))

                def body(E, tc=tc, bank=bank, gA=gA, gB=gB):
                    ins = None
                    for f in range(FC):
                        buf = gA[0] if f < 8 else gB[0]
                        ins = E.matmul(bank[:], h_sb[:, f, tc * P:(tc + 1) * P],
                                       wpair[buf][:, (f % 8) * 512:(f % 8 + 1) * 512],
                                       start=(f == 0), stop=(f == FC - 1))
                    return ins
                tok = task("tensor", waits, body, incs=[("pe", 1)])
                lastok = tok
                ev = task("vector", [("pe", tok["pe"])],
                          lambda E, tc=tc, cb=cb, bank=bank: E.tensor_tensor(
                              x1[:, tc, cb * 512:(cb + 1) * 512], bank[:],
                              x1[:, tc, cb * 512:(cb + 1) * 512], ALU.add),
                          incs=[("dve", 1)])
                setrd(("pc", bi), ev)
                if e == NEXP - 1 and cb == 3:
                    T_out[tc] = ev["dve"]
            setrd(("wpair", gA[0]), lastok)
            setrd(("wpair", gB[0]), lastok)

    for tc in range(OTC):
        task("sync", [("dve", T_out[tc]), ("odma", B.at("odma"))],
             lambda E, tc=tc: E.dma_start(out=d["out"][tc * P:(tc + 1) * P, :], in_=x1[:, tc, :]),
             incs=[("odma", 16)])

    stopped[0] = False
    if taps:
        tw = [(s, B.at(s)) for s in ("pe", "dve", "act")]
        for name in taps:
            t = tap_reg[name]
            dt_ = nc.dram_tensor("tap_" + name, list(t.shape), t.dtype,
                                 kind="ExternalOutput").ap()
            task("sync", tw + [("odma", B.at("odma"))],
                 lambda E, t=t, dt_=dt_: E.dma_start(out=dt_[:], in_=t[:]),
                 incs=[("odma", 16)])
    task("sync", [("odma", B.at("odma"))], lambda E: E.nop())

    for s in ["xdma", "xdma0", "xdma1", "wdma0", "wdma1", "wdma2",
              "odma", "pe", "dve", "act"]:
        nc.gpsimd.sem_clear(B.S[s])
    nc.all_engine_barrier()

    with nc.Block() as block:
        @block.sync
        def _(E):
            for f in B.ops["sync"]:
                f(E)

        @block.tensor
        def _(E):
            for f in B.ops["tensor"]:
                f(E)

        @block.vector
        def _(E):
            for f in B.ops["vector"]:
                f(E)

        @block.scalar
        def _(E):
            for f in B.ops["scalar"]:
                f(E)

        @block.gpsimd
        def _(E):
            for f in B.ops["gpsimd"]:
                f(E)

    stack.close()
    return nc


def kernel(**inputs):
    shared, per_core = _host_prep(inputs)
    nc = bass.Bass("TRN2", target_bir_lowering=False)
    build_program(nc)
    in_maps = []
    for c in range(N_CORES):
        m = dict(per_core[c])
        m.update(shared)
        in_maps.append(m)
    res = run_bass_kernel_spmd(nc, in_maps, core_ids=list(range(N_CORES)))
    out = np.concatenate([np.asarray(r["out"], dtype=np.float32) for r in res.results], axis=0)
    return np.ascontiguousarray(out.reshape(1, T_FULL, HID))


if __name__ == "__main__":
    import jax
    jax.config.update('jax_platforms', 'cpu')
    import reference as R
    inp = R.setup_inputs(0)
    inp = {k: np.asarray(v) for k, v in inp.items()}
    out = kernel(**inp)
    ref = np.asarray(R.reference(**inp))
    num = float(((out - ref) ** 2).mean())
    den = float((ref ** 2).mean())
    print(f"rvr: {num/den:.3e}")


# version 09 (final)
# speedup vs baseline: 1.0111x; 1.0111x over previous
import numpy as np
import ml_dtypes
import concourse.bass as bass
import concourse.mybir as mybir
from concourse.bass_utils import run_bass_kernel_spmd

F32 = mybir.dt.float32
BF16 = mybir.dt.bfloat16
AF = mybir.ActivationFunctionType
ALU = mybir.AluOpType
AX = mybir.AxisListType
BF = ml_dtypes.bfloat16

P = 128
HID = 2048
CC = HID // P
QH, KVH, HD, WINDOW = 16, 4, 128, 512
T_FULL, N_CORES = 8192, 8
OWN = T_FULL // N_CORES
HALO = WINDOW
LOC = OWN + HALO
OTC, LTC = OWN // P, LOC // P
F_MOE = 1408
FC = F_MOE // P
NEXP = 12
EPS = 1e-5
NEGV = np.float32(-1e30)

ENGINES = ("sync", "tensor", "vector", "scalar", "gpsimd")


class _Stop(Exception):
    pass



def _lhsT_stream(W):
    M, K = W.shape
    t = np.ascontiguousarray(W.T).reshape(K // P, P, M // P, P).transpose(2, 1, 0, 3)
    return np.ascontiguousarray(t).reshape(M // P, P, K)


def _rhs_cols_stream(W, cbw):
    M, K = W.shape
    t = np.ascontiguousarray(W.T).reshape(K // P, P, M // cbw, cbw).transpose(2, 0, 1, 3)
    return np.ascontiguousarray(t)


def _host_prep(inputs):
    x = np.asarray(inputs["L0_attn_x"], dtype=np.float32)[0]
    wq = np.asarray(inputs["L0_attn_wq"], dtype=np.float32)
    wk = np.asarray(inputs["L0_attn_wk"], dtype=np.float32)
    wv = np.asarray(inputs["L0_attn_wv"], dtype=np.float32)
    wo = np.asarray(inputs["L0_attn_wo"], dtype=np.float32)
    gate_w = np.asarray(inputs["L0_ffn_gate_w"], dtype=np.float32)
    efc1 = np.asarray(inputs["L0_ffn_expert_fc1"], dtype=np.float32)
    efc2 = np.asarray(inputs["L0_ffn_expert_fc2"], dtype=np.float32)
    eproj = np.asarray(inputs["L0_ffn_expert_proj"], dtype=np.float32)
    sfc1 = np.asarray(inputs["L0_ffn_shared_fc1"], dtype=np.float32)
    sfc2 = np.asarray(inputs["L0_ffn_shared_fc2"], dtype=np.float32)
    sproj = np.asarray(inputs["L0_ffn_shared_proj"], dtype=np.float32)
    sgw = np.asarray(inputs["L0_ffn_shared_gate_w"], dtype=np.float32)

    shared = {}
    shared["wq_s"] = _lhsT_stream(wq)
    shared["wk_s"] = _lhsT_stream(wk)
    shared["wv_s"] = np.ascontiguousarray(wv.T).reshape(CC, P, 512)
    shared["wo_s"] = _rhs_cols_stream(wo, 512)
    rtr9 = np.concatenate([gate_w, sgw], axis=0)
    r = np.ascontiguousarray(rtr9.T).reshape(CC, P, 9).transpose(1, 0, 2)
    shared["rtr9_s"] = np.ascontiguousarray(r).reshape(1, P, CC * 9)

    fc12 = np.empty((NEXP, 2 * FC, P, HID), dtype=np.float32)
    proj = np.empty((NEXP, 4, FC, P, 512), dtype=np.float32)
    for e in range(8):
        fc12[e, 0::2] = _lhsT_stream(efc1[e])
        fc12[e, 1::2] = _lhsT_stream(efc2[e])
        proj[e] = _rhs_cols_stream(eproj[e], 512)
    for c in range(4):
        sl = slice(F_MOE * c, F_MOE * (c + 1))
        fc12[8 + c, 0::2] = _lhsT_stream(np.ascontiguousarray(sfc1[sl]))
        fc12[8 + c, 1::2] = _lhsT_stream(np.ascontiguousarray(sfc2[sl]))
        proj[8 + c] = _rhs_cols_stream(np.ascontiguousarray(sproj[:, sl]), 512)
    shared["fc12_s"] = fc12
    shared["proj_s"] = proj

    shared["idn_bf"] = np.eye(P, dtype=np.float32).astype(BF)
    shared["idn_f32"] = np.eye(P, dtype=np.float32)
    oh9 = np.zeros((9, 9 * P), dtype=np.float32)
    for e in range(9):
        oh9[e, e * P:(e + 1) * P] = 1.0
    shared["oh9"] = oh9

    half = HD // 2
    inv = 1.0 / 10000.0 ** (np.arange(half, dtype=np.float32) / half)
    tt = np.arange(T_FULL, dtype=np.float32)
    fr = np.outer(tt, inv)
    cosT, sinT = np.cos(fr).astype(np.float32), np.sin(fr).astype(np.float32)
    scale = np.float32(1.0 / np.sqrt(HD))

    per_core = []
    for c in range(N_CORES):
        base = OWN * c - HALO
        xs = np.zeros((LOC, HID), dtype=np.float32)
        lo = max(base, 0)
        xs[lo - base:] = x[lo:base + LOC]
        idx = np.clip(np.arange(LOC) + base, 0, T_FULL - 1)
        kcos = np.ascontiguousarray(cosT[idx].T)
        ksin = np.ascontiguousarray(sinT[idx].T)
        qcos = np.ascontiguousarray(kcos[:, HALO:] * scale)
        qsin = np.ascontiguousarray(ksin[:, HALO:] * scale)
        m = np.full((P, OTC, 5, P), NEGV, dtype=np.float32)
        for qt in range(OTC):
            for cc5 in range(5):
                j0 = cc5 * P + np.arange(P)[:, None]
                q_ = np.arange(P)[None, :]
                r_ = qt * P + j0
                ok = (j0 >= q_) & (j0 <= q_ + WINDOW) & (r_ >= HALO - OWN * c)
                m[:, qt, cc5, :] = np.where(ok, 0.0, NEGV)
        per_core.append({
            "x": xs,
            "masks": np.ascontiguousarray(m.reshape(P, OTC * 5 * P)).astype(BF),
            "qcos": qcos, "qsin": qsin, "kcos": kcos, "ksin": ksin,
        })
    return shared, per_core



class Builder:
    def __init__(self, nc):
        self.nc = nc
        self.ops = {e: [] for e in ENGINES}
        self.cnt = {}
        self.S = {}
        self.war = {}

    def sem(self, name, stack):
        self.S[name] = stack.enter_context(self.nc.semaphore(name))
        self.cnt[name] = 0

    def inc(self, name, n=1):
        self.cnt[name] += n
        return self.cnt[name]

    def at(self, name):
        return self.cnt[name]


def _emit_waits(E, S, waits):
    seen = {}
    for s, v in waits:
        if v > 0 and seen.get(s, -1) < v:
            seen[s] = v
    for s, v in seen.items():
        E.wait_ge(S[s], v)


def build_program(nc, phases=9, taps=()):
    from contextlib import ExitStack
    stack = ExitStack()

    d = {}
    d["x"] = nc.dram_tensor("x", [LOC, HID], F32, kind="ExternalInput").ap()
    d["masks"] = nc.dram_tensor("masks", [P, OTC * 5 * P], BF16, kind="ExternalInput").ap()
    for n, w in [("qcos", OWN), ("qsin", OWN), ("kcos", LOC), ("ksin", LOC)]:
        d[n] = nc.dram_tensor(n, [64, w], F32, kind="ExternalInput").ap()
    d["idn_bf"] = nc.dram_tensor("idn_bf", [P, P], BF16, kind="ExternalInput").ap()
    d["idn_f32"] = nc.dram_tensor("idn_f32", [P, P], F32, kind="ExternalInput").ap()
    d["oh9"] = nc.dram_tensor("oh9", [9, 9 * P], F32, kind="ExternalInput").ap()
    d["wq_s"] = nc.dram_tensor("wq_s", [16, P, HID], F32, kind="ExternalInput").ap()
    d["wk_s"] = nc.dram_tensor("wk_s", [4, P, HID], F32, kind="ExternalInput").ap()
    d["wv_s"] = nc.dram_tensor("wv_s", [CC, P, 512], F32, kind="ExternalInput").ap()
    d["wo_s"] = nc.dram_tensor("wo_s", [4, 16, P, 512], F32, kind="ExternalInput").ap()
    d["rtr9_s"] = nc.dram_tensor("rtr9_s", [1, P, CC * 9], F32, kind="ExternalInput").ap()
    d["fc12_s"] = nc.dram_tensor("fc12_s", [NEXP, 2 * FC, P, HID], F32, kind="ExternalInput").ap()
    d["proj_s"] = nc.dram_tensor("proj_s", [NEXP, 4, FC, P, 512], F32, kind="ExternalInput").ap()
    d["out"] = nc.dram_tensor("out", [OWN, HID], F32, kind="ExternalOutput").ap()

    KB = 1024
    SBASE = 16512
    cur = [0]

    nc._salloc_map = {}

    def salloc(name, shape, dtype, at=None):
        if at is not None:
            cur[0] = at
        t = nc.alloc_sbuf_tensor_at(name, shape, dtype, offset=SBASE + cur[0])
        nbytes = int(np.prod(shape[1:])) * mybir.dt.size(dtype)
        nc._salloc_map[name] = (cur[0], nbytes)
        cur[0] += (nbytes + 31) // 32 * 32
        return t

    idn_bf = salloc("idn_bf_s", [P, P], BF16)
    idn_f32 = salloc("idn_f32_s", [P, P], F32)
    ones_b = salloc("ones_b", [P, 1], BF16)
    ones1 = salloc("ones1", [1, P], F32)
    wpair = [salloc(f"wpair{i}", [P, 4096], BF16) for i in range(3)]
    xn_tm0 = salloc("xn_tm", [P, HID], BF16)
    s1v = salloc("s1v", [P, 1], F32)
    s2v = salloc("s2v", [P, 1], F32)
    negm = salloc("negm", [P, 1], F32)
    varv = salloc("varv", [P, 1], F32)
    rstd = salloc("rstd", [P, 1], F32)
    nmr = salloc("nmr", [P, 1], F32)
    r_sb = [salloc(f"rsb{i}", [1, P], F32) for i in range(2)]
    rb_sb = [salloc(f"rbsb{i}", [P, P], F32) for i in range(2)]
    r9_tm = salloc("r9_tm", [P, OTC, 9], F32)
    cg_tm = salloc("cg_tm", [P, OTC, 9], F32)
    e_sb = salloc("e_sb", [P, 8], F32)
    pr_sb = salloc("pr_sb", [P, 8], F32)
    msk_sb = salloc("msk_sb", [P, 8], F32)
    ge_sb = salloc("ge_sb", [P, 8], F32)
    mx1 = salloc("mx1", [P, 1], F32)
    sden = salloc("sden", [P, 1], F32)
    rs9 = salloc("rs9", [P, 1], F32)
    m2v = salloc("m2v", [P, 1], F32)
    rtr9_sb = salloc("rtr9_sb", [P, CC * 9], BF16)
    s1v2 = salloc("s1v2", [P, 1], F32)
    s2v2 = salloc("s2v2", [P, 1], F32)
    negm2 = salloc("negm2", [P, 1], F32)
    varv2 = salloc("varv2", [P, 1], F32)
    rstd2 = salloc("rstd2", [P, 1], F32)
    nmr2 = salloc("nmr2", [P, 1], F32)
    O1 = (cur[0] + 31) // 32 * 32

    xn1 = salloc("xn1", [P, CC, LOC], BF16, at=O1)
    v_tm = salloc("v_tm", [P, LTC, 512], BF16)
    xn_tm2 = salloc("xn_tm2", [P, HID], BF16)
    x1 = salloc("x1", [P, OTC, HID], F32, at=O1)
    O2 = O1 + 65536

    qcos = salloc("qcos_s", [64, OWN], F32, at=O2)
    qsin = salloc("qsin_s", [64, OWN], F32)
    kcos = salloc("kcos_s", [64, LOC], F32)
    ksin = salloc("ksin_s", [64, LOC], F32)
    masks = salloc("masks_s", [P, OTC, 5, P], BF16)
    q_rope = salloc("q_rope", [P, QH, OWN], BF16)
    xn2 = salloc("xn2", [P, CC, OWN], BF16, at=O2)
    r9_fm = salloc("r9_fm", [9, OWN], F32)
    cg_fm = salloc("cg_fm", [9, OWN], F32)
    oh9_sb = salloc("oh9_sb", [9, 9 * P], F32)
    O3 = O2 + 63488

    attn_fm = salloc("attn_fm", [P, QH, OWN], BF16, at=O3)
    h_sb = salloc("h_sb", [P, FC, OWN], BF16, at=O3)
    cbc = [salloc(f"cbc{i}", [P, OWN], F32) for i in range(2)]
    O4 = O3 + 32768

    xtile = [salloc(f"xtile{i}", [P, HID], F32, at=O4 if i == 0 else None)
             for i in range(2)]
    assert cur[0] <= 212863, cur[0]
    k_rope = salloc("k_rope", [P, KVH, LOC], BF16, at=O4)
    rp = [salloc(f"rp{i}", [64, 512], F32, at=O3 if i == 0 else None)
          for i in range(2)]
    probsT = [salloc(f"probsT{i}", [P, 640], BF16, at=O4 + 12288 if i == 0 else None)
              for i in range(2)]
    assert cur[0] <= 212863, cur[0]
    silu_sb = [salloc(f"silu{i}", [P, 512], F32, at=O4 if i == 0 else None)
               for i in range(2)]
    t1_sb = [salloc(f"t1sb{i}", [P, 512], F32) for i in range(2)]
    assert cur[0] <= 212863, cur[0]

    pb = [stack.enter_context(nc.psum_tensor(f"pb{i}", [P, 512], F32)) for i in range(4)]
    pc = [stack.enter_context(nc.psum_tensor(f"pc{i}", [P, 512], F32)) for i in range(2)]
    pd = stack.enter_context(nc.psum_tensor("pd", [P, 512], F32))
    ptr = stack.enter_context(nc.psum_tensor("ptr", [P, 1024], BF16))

    B = Builder(nc)
    for s in ["xdma", "wdma", "odma", "pe", "dve", "act"]:
        B.sem(s, stack)

    stopped = [False]

    def ckpt(k):
        if phases <= k:
            stopped[0] = True

    import os
    SERIAL = os.environ.get("K_SERIAL", "0")

    def task(eng, waits, body, incs=()):
        if stopped[0]:
            import collections
            return collections.defaultdict(int)
        w = list(waits)
        if SERIAL == "1" or SERIAL == eng:
            w = [(s, B.at(s)) for s in ("pe", "dve", "act")] + \
                [(s, B.at(s)) for s in ("xdma", "wdma", "odma")]
        ic = [(s, n, B.inc(s, n)) for s, n in incs]

        def run(E, w=w, ic=ic, body=body):
            _emit_waits(E, B.S, w)
            ins = body(E)
            for s, n, _ in ic:
                ins = ins.then_inc(B.S[s], n)
        B.ops[eng].append(run)
        return {s: v for s, n, v in ic}

    def rd(res):
        return B.war.get(res, [])

    def setrd(res, tok):
        B.war[res] = [(s, v) for s, v in tok.items()]

    tap_reg = {"xn1": xn1, "krope": k_rope, "qrope": q_rope, "vtm": v_tm,
               "attn": attn_fm, "x1": x1, "xn2": xn2, "cgtm": cg_tm, "cgfm": cg_fm,
               "rtm": r9_tm, "h": h_sb, "probs0": probsT[0], "probs1": probsT[1]}

    task("vector", [], lambda E: E.memset(ones_b[:], 1.0))
    task("vector", [], lambda E: E.memset(ones1[:], 1.0), incs=[("dve", 1)])
    for name, dst in [("idn_bf", idn_bf), ("idn_f32", idn_f32), ("masks", masks),
                      ("qcos", qcos), ("qsin", qsin), ("kcos", kcos), ("ksin", ksin)]:
        task("sync", [], lambda E, dst=dst, name=name: E.dma_start(out=dst[:], in_=d[name][:]),
             incs=[("xdma", 16)])
    T_const = B.at("xdma")

    ring_g = [0]

    def grouped(dram3, s0, nslab, F):
        g = ring_g[0]
        ring_g[0] += 1
        buf = g % 3
        waits = rd(("wpair", buf))
        if nslab == 1:
            src = dram3[s0]
            dst = wpair[buf][:, 0:F]
        else:
            src = dram3[s0:s0 + nslab].rearrange("s p f -> p s f")
            dst = wpair[buf][:, 0:nslab * F].rearrange("p (s f) -> p s f", f=F)
        task("gpsimd", waits,
             lambda E, src=src, dst=dst: E.dma_start(out=dst, in_=src),
             incs=[("wdma", 16)])
        return buf, B.at("wdma")

    stats_sets = [(s1v, s2v, negm, varv, rstd, nmr), (s1v2, s2v2, negm2, varv2, rstd2, nmr2)]

    def ln_tile(src_ap, wait_src, dst_fm, tcol, src_tok=None, alt=0, xbuf=None):
        s1v, s2v, negm, varv, rstd, nmr = stats_sets[alt]
        xn_tm = xbuf if xbuf is not None else xn_tm0
        t0 = task("vector", wait_src,
                  lambda E, s1v=s1v: E.reduce_sum(s1v[:], src_ap, axis=AX.X),
                  incs=[("dve", 1)])
        xkey = ("xn_tm", xn_tm.name)
        ta = task("scalar", wait_src + rd(xkey),
                  lambda E, xn_tm=xn_tm, s2v=s2v: E.activation(xn_tm[:], src_ap, AF.Square,
                                                               accum_out=s2v[:]),
                  incs=[("act", 1)])
        t1 = task("vector", [("dve", t0["dve"])],
                  lambda E, negm=negm, s1v=s1v: E.tensor_scalar(negm[:], s1v[:],
                                                               float(-1.0 / HID), None, ALU.mult),
                  incs=[("dve", 1)])
        t2 = task("vector", [("dve", t1["dve"])],
                  lambda E, varv=varv, negm=negm: E.tensor_tensor(varv[:], negm[:], negm[:],
                                                                 ALU.mult),
                  incs=[("dve", 1)])
        t3 = task("vector", [("dve", t2["dve"]), ("act", ta["act"])],
                  lambda E, varv=varv, s2v=s2v: E.scalar_tensor_tensor(
                      varv[:], s2v[:], float(1.0 / HID), varv[:], ALU.mult, ALU.subtract),
                  incs=[("dve", 1)])
        tv = task("vector", [("dve", t3["dve"])],
                  lambda E, varv=varv: E.tensor_scalar(varv[:], varv[:], float(EPS), None,
                                                       ALU.add),
                  incs=[("dve", 1)])
        tl = task("scalar", [("dve", tv["dve"])],
                  lambda E, varv=varv: E.activation(varv[:], varv[:], AF.Ln),
                  incs=[("act", 1)])
        tv2 = task("vector", [("act", tl["act"])],
                   lambda E, varv=varv: E.tensor_scalar(varv[:], varv[:], -0.5, None, ALU.mult),
                   incs=[("dve", 1)])
        ta2 = task("scalar", [("dve", tv2["dve"])],
                   lambda E, rstd=rstd, varv=varv: E.activation(rstd[:], varv[:], AF.Exp),
                   incs=[("act", 1)])
        t4 = task("vector", [("act", ta2["act"]), ("dve", t1["dve"])],
                  lambda E, nmr=nmr, negm=negm, rstd=rstd: E.tensor_tensor(nmr[:], negm[:],
                                                                           rstd[:], ALU.mult),
                  incs=[("dve", 1)])
        tn = task("vector", [("dve", t4["dve"])] + rd(xkey),
                  lambda E, xn_tm=xn_tm, nmr=nmr, rstd=rstd: E.tensor_scalar(
                      xn_tm[:], src_ap, nmr[:, 0:1], rstd[:, 0:1], ALU.add, ALU.mult),
                  incs=[("dve", 1)])
        ev = None
        tp = None
        for gph in range(2):
            def tbody(E, gph=gph, xn_tm=xn_tm):
                ins = None
                for j in range(8):
                    cc = gph * 8 + j
                    ins = E.transpose(ptr[:, j * P:(j + 1) * P], xn_tm[:, cc * P:(cc + 1) * P],
                                      idn_bf[:])
                return ins
            tp = task("tensor", [("dve", tn["dve"]), ("xdma", T_const)] + rd(("ptr",)), tbody,
                      incs=[("pe", 1)])
            ev = task("vector", [("pe", tp["pe"])],
                      lambda E, gph=gph, tcol=tcol: E.tensor_copy(
                          dst_fm[:, gph * 8:(gph + 1) * 8, tcol:tcol + P],
                          ptr[:].rearrange("p (c q) -> p c q", q=P)),
                      incs=[("dve", 1)])
            setrd(("ptr",), ev)
        setrd(xkey, tp)
        if src_tok is not None:
            src_tok.clear()
            src_tok.update({"dve": tn["dve"], "act": ta["act"]})
        return ev

    T_xn1 = {}

    def do_ln1_tile(t):
        xb = t % 2
        xw = list(rd(("xtile", xb)))
        tok_dma = task("sync", xw,
                       lambda E, t=t, xb=xb: E.dma_start(out=xtile[xb][:],
                                                         in_=d["x"][t * P:(t + 1) * P, :]),
                       incs=[("xdma", 16)])
        stok = {}
        ev = ln_tile(xtile[xb][:], [("xdma", tok_dma["xdma"])], xn1, t * P, src_tok=stok,
                     alt=t % 2, xbuf=(xn_tm0 if t % 2 == 0 else xn_tm2))
        setrd(("xtile", xb), stok)
        T_xn1[t] = ev["dve"]

    punit = [0]

    def fm_proj_unit(slab_fn, rhs_fn, wait_extra):
        u = punit[0]
        punit[0] += 1
        bi = u % 4
        bank = pb[bi]

        def body(E, bank=bank):
            ins = None
            for kc in range(CC):
                ins = E.matmul(bank[:], slab_fn(kc), rhs_fn(kc),
                               start=(kc == 0), stop=(kc == CC - 1))
            return ins
        tok = task("tensor", wait_extra + rd(("pb", bi)), body, incs=[("pe", 1)])
        return bank, bi, tok

    def rope_evac(bank, tok, cos_t, sin_t, th0, w, dst, head, c0):
        task("vector", [("pe", tok["pe"])],
             lambda E: E.tensor_tensor(rp[0][:, 0:w], bank[0:64, 0:w], cos_t[:, th0:th0 + w], ALU.mult))
        task("vector", [],
             lambda E: E.tensor_tensor(rp[1][:, 0:w], bank[64:128, 0:w], sin_t[:, th0:th0 + w], ALU.mult))
        task("vector", [],
             lambda E: E.tensor_tensor(dst[0:64, head, c0:c0 + w], rp[0][:, 0:w], rp[1][:, 0:w],
                                       ALU.subtract))
        task("vector", [],
             lambda E: E.tensor_tensor(rp[0][:, 0:w], bank[0:64, 0:w], sin_t[:, th0:th0 + w], ALU.mult))
        task("vector", [],
             lambda E: E.tensor_tensor(rp[1][:, 0:w], bank[64:128, 0:w], cos_t[:, th0:th0 + w], ALU.mult))
        return task("vector", [],
                    lambda E: E.tensor_tensor(dst[64:128, head, c0:c0 + w], rp[0][:, 0:w],
                                              rp[1][:, 0:w], ALU.add),
                    incs=[("dve", 1)])

    wk_groups = [grouped(d["wk_s"], gi * 2, 2, HID) for gi in range(2)]
    for t in range(LTC):
        do_ln1_tile(t)
    T_xn1done = B.at("dve")
    ckpt(1)
    klast = None
    kunits = [(mc, th) for th in range(3) for mc in range(4)
              if not (mc == 3 or (mc == 2 and th == 2))]
    kunits += [(3, 0), (3, 1), (2, 2), (3, 2)]
    for mc, th in kunits:
        buf, wtok = wk_groups[mc // 2]
        s = mc % 2
        late = (mc == 3 or (mc == 2 and th == 2))
        gate = T_xn1done if late else T_xn1[th * 4 + 3]
        bank, bi, tok = fm_proj_unit(
            lambda kc, buf=buf, s=s: wpair[buf][:, s * HID + kc * P:s * HID + (kc + 1) * P],
            lambda kc, th=th: xn1[:, kc, th * 512:(th + 1) * 512],
            [("wdma", wtok), ("dve", gate)])
        klast = tok
        ev = rope_evac(bank, tok, kcos, ksin, th * 512, 512, k_rope, mc, th * 512)
        setrd(("pb", bi), ev)
    setrd(("wpair", wk_groups[0][0]), klast)
    setrd(("wpair", wk_groups[1][0]), klast)
    wv_groups = [grouped(d["wv_s"], gi * 8, 8, 512) for gi in range(2)]
    vtok = None
    for tc in range(LTC):
        u = punit[0]
        punit[0] += 1
        bi = u % 4
        bank = pb[bi]

        def vbody(E, tc=tc, bank=bank):
            ins = None
            for kc in range(CC):
                buf = wv_groups[kc // 8][0]
                ins = E.matmul(bank[:], xn1[:, kc, tc * P:(tc + 1) * P],
                               wpair[buf][:, (kc % 8) * 512:(kc % 8 + 1) * 512],
                               start=(kc == 0), stop=(kc == CC - 1))
            return ins
        wts = [("wdma", wv_groups[1][1]), ("dve", T_xn1[tc])]
        vtok = task("tensor", wts + rd(("pb", bi)), vbody, incs=[("pe", 1)])
        ev = task("vector", [("pe", vtok["pe"])],
                  lambda E, tc=tc, bank=bank: E.tensor_copy(v_tm[:, tc, :], bank[:]),
                  incs=[("dve", 1)])
        setrd(("pb", bi), ev)
    setrd(("wpair", wv_groups[0][0]), vtok)
    setrd(("wpair", wv_groups[1][0]), vtok)
    for gi in range(8):
        buf, wtok = grouped(d["wq_s"], gi * 2, 2, HID)
        last = None
        for s in range(2):
            mc = gi * 2 + s
            for th in range(2):
                bank, bi, tok = fm_proj_unit(
                    lambda kc, buf=buf, s=s: wpair[buf][:, s * HID + kc * P:s * HID + (kc + 1) * P],
                    lambda kc, th=th: xn1[:, kc, HALO + th * 512:HALO + (th + 1) * 512],
                    [("wdma", wtok)])
                last = tok
                ev = rope_evac(bank, tok, qcos, qsin, th * 512, 512, q_rope, mc, th * 512)
                setrd(("pb", bi), ev)
        setrd(("wpair", buf), last)
    T_p2done = B.at("dve")
    T_p2pe = B.at("pe")
    ckpt(2)

    pairs = [(h, qt) for h in range(QH) for qt in range(OTC)]
    T_sc, T_exp, T_dn, T_pv, T_rbc, T_r, T_rbev, T_attn = {}, {}, {}, {}, {}, {}, {}, {}

    def emit_sc(i):
        h, qt = pairs[i]
        g = h // (QH // KVH)
        sa, sb = pb[2 * (i % 2)], pb[2 * (i % 2) + 1]
        waits = [("dve", T_p2done), ("xdma", T_const)]
        if i - 2 >= 0:
            waits.append(("act", T_exp[i - 2]))

        def body(E, h=h, qt=qt, g=g, sa=sa, sb=sb):
            ins = None
            for cc5 in range(4):
                ins = E.matmul(sa[:, cc5 * P:(cc5 + 1) * P], idn_bf[:], masks[:, qt, cc5, :],
                               start=(cc5 == 0), stop=False)
            for cc5 in range(4):
                ins = E.matmul(sa[:, cc5 * P:(cc5 + 1) * P],
                               k_rope[:, g, (qt + cc5) * P:(qt + cc5 + 1) * P],
                               q_rope[:, h, qt * P:(qt + 1) * P],
                               start=False, stop=(cc5 == 3))
            ins = E.matmul(sb[:, 0:P], idn_bf[:], masks[:, qt, 4, :], start=True, stop=False)
            ins = E.matmul(sb[:, 0:P], k_rope[:, g, (qt + 4) * P:(qt + 5) * P],
                           q_rope[:, h, qt * P:(qt + 1) * P], start=False, stop=True)
            return ins
        T_sc[i] = task("tensor", waits, body, incs=[("pe", 1)])["pe"]

    def emit_exp(i):
        waits = [("pe", T_sc[i])]
        if i - 2 >= 0:
            waits.append(("pe", T_pv[i - 2]))
        sa, sb = pb[2 * (i % 2)], pb[2 * (i % 2) + 1]
        task("scalar", waits,
             lambda E, i=i, sa=sa: E.activation(probsT[i % 2][:, 0:512], sa[:], AF.Exp))
        T_exp[i] = task("scalar", [],
                        lambda E, i=i, sb=sb: E.activation(probsT[i % 2][:, 512:640], sb[:, 0:P],
                                                           AF.Exp),
                        incs=[("act", 1)])["act"]

    def emit_dn(i):
        waits = [("act", T_exp[i])]
        if i - 1 >= 0:
            waits.append(("dve", T_rbev[i - 1]))

        def body(E, i=i):
            ins = None
            for cc5 in range(5):
                ins = E.matmul(pd[0:1, 0:P], ones_b[:], probsT[i % 2][:, cc5 * P:(cc5 + 1) * P],
                               start=(cc5 == 0), stop=(cc5 == 4))
            return ins
        T_dn[i] = task("tensor", waits, body, incs=[("pe", 1)])["pe"]

    def emit_recip(i):
        waits = [("pe", T_dn[i])]
        if i - 2 >= 0:
            waits.append(("pe", T_rbc[i - 2]))
        T_r[i] = task("vector", waits,
                      lambda E, i=i: E.reciprocal(r_sb[i % 2][:], pd[0:1, 0:P]),
                      incs=[("dve", 1)])["dve"]

    def emit_pv(i):
        h, qt = pairs[i]
        g = h // (QH // KVH)
        waits = [("act", T_exp[i])]
        if i - 2 >= 0:
            waits.append(("dve", T_attn[i - 2]))

        def body(E, i=i, g=g, qt=qt):
            ins = None
            for cc5 in range(5):
                ins = E.matmul(pc[i % 2][:, 0:P],
                               v_tm[:, qt + cc5, g * P:(g + 1) * P],
                               probsT[i % 2][:, cc5 * P:(cc5 + 1) * P],
                               start=(cc5 == 0), stop=(cc5 == 4))
            return ins
        T_pv[i] = task("tensor", waits, body, incs=[("pe", 1)])["pe"]

    def emit_rbc(i):
        T_rbc[i] = task("tensor", [("dve", T_r[i])],
                        lambda E, i=i: E.matmul(pd[:, P:2 * P], ones1[:], r_sb[i % 2][:],
                                                start=True, stop=True),
                        incs=[("pe", 1)])["pe"]

    def emit_attn_evac(i):
        h, qt = pairs[i]
        T_rbev[i] = task("vector", [("pe", T_rbc[i]), ("pe", T_pv[i])],
                         lambda E, i=i: E.tensor_copy(rb_sb[i % 2][:], pd[:, P:2 * P]),
                         incs=[("dve", 1)])["dve"]
        T_attn[i] = task("vector", [("dve", T_rbev[i])],
                         lambda E, i=i, h=h, qt=qt: E.tensor_tensor(
                             attn_fm[:, h, qt * P:(qt + 1) * P], pc[i % 2][:, 0:P],
                             rb_sb[i % 2][:], ALU.mult),
                         incs=[("dve", 1)])["dve"]

    emit_sc(0)
    emit_exp(0)
    emit_sc(1)
    for i in range(128):
        if i >= 1:
            emit_exp(i)
        emit_dn(i)
        emit_recip(i)
        emit_pv(i)
        emit_rbc(i)
        emit_attn_evac(i)
        if i + 2 < 128:
            emit_sc(i + 2)
    B.war[("pb", 0)] = [("act", T_exp[126])]
    B.war[("pb", 1)] = [("act", T_exp[126])]
    B.war[("pb", 2)] = [("act", T_exp[127])]
    B.war[("pb", 3)] = [("act", T_exp[127])]
    B.war[("pd",)] = [("dve", T_rbev[127])]
    B.war[("pc", 0)] = [("dve", T_attn[126])]
    B.war[("pc", 1)] = [("dve", T_attn[127])]
    T_attn_all = T_attn[127]
    T_pv_all = T_pv[127]
    ckpt(3)

    T_x1ld = {}
    for tc in range(OTC):
        gate = T_p2pe if tc <= 5 else T_pv_all
        T_x1ld[tc] = task("sync", [("pe", gate)],
                          lambda E, tc=tc: E.dma_start(out=x1[:, tc, :],
                                                       in_=d["x"][(4 + tc) * P:(5 + tc) * P, :]),
                          incs=[("xdma", 16)])["xdma"]
    u4 = [0]
    T_x1 = {}
    for hb in range(4):
        gA = grouped(d["wo_s"][hb], 0, 8, 512)
        gB = grouped(d["wo_s"][hb], 8, 8, 512)
        last = None
        for tc in range(OTC):
            u = u4[0]
            u4[0] += 1
            bi = u % 4
            bank = pb[bi]
            waits = [("dve", T_attn_all), ("wdma", gB[1])] + rd(("pb", bi))

            def body(E, tc=tc, bank=bank, gA=gA, gB=gB):
                ins = None
                for ac in range(CC):
                    buf = gA[0] if ac < 8 else gB[0]
                    ins = E.matmul(bank[:],
                                   attn_fm[:, ac, tc * P:(tc + 1) * P],
                                   wpair[buf][:, (ac % 8) * 512:(ac % 8 + 1) * 512],
                                   start=(ac == 0), stop=(ac == CC - 1))
                return ins
            tok = task("tensor", waits, body, incs=[("pe", 1)])
            last = tok
            ev = task("vector", [("pe", tok["pe"]), ("xdma", T_x1ld[tc])],
                      lambda E, tc=tc, hb=hb, bank=bank: E.tensor_tensor(
                          x1[:, tc, hb * 512:(hb + 1) * 512], bank[:],
                          x1[:, tc, hb * 512:(hb + 1) * 512], ALU.add),
                      incs=[("dve", 1)])
            setrd(("pb", bi), ev)
            if hb == 3:
                T_x1[tc] = ev["dve"]
        setrd(("wpair", gA[0]), last)
        setrd(("wpair", gB[0]), last)

    ckpt(4)

    for tc in range(OTC):
        ev = ln_tile(x1[:, tc, :], [("dve", T_x1[tc])], xn2, tc * P)
    T_xn2done = B.at("dve")

    oh_tok = task("sync", [("pe", T_sc[127])],
                  lambda E: E.dma_start(out=oh9_sb[:], in_=d["oh9"][:]), incs=[("xdma", 16)])
    rbuf, rtok = grouped(d["rtr9_s"], 0, 1, CC * 9)
    trc = task("vector", [("wdma", rtok)],
               lambda E: E.tensor_copy(rtr9_sb[:], wpair[rbuf][:, 0:CC * 9]),
               incs=[("dve", 1)])
    B.war[("wpair", rbuf)] = [("dve", trc["dve"])]
    T_rtrcp = trc["dve"]

    for th in range(2):
        bank = pb[th]

        def body(E, th=th, bank=bank):
            ins = None
            for kc in range(CC):
                ins = E.matmul(bank[0:9, :], rtr9_sb[:, kc * 9:(kc + 1) * 9],
                               xn2[:, kc, th * 512:(th + 1) * 512],
                               start=(kc == 0), stop=(kc == CC - 1))
            return ins
        tok = task("tensor", [("dve", max(T_xn2done, T_rtrcp))] + rd(("pb", th)), body,
                   incs=[("pe", 1)])
        ev = task("vector", [("pe", tok["pe"])],
                  lambda E, th=th, bank=bank: E.tensor_copy(r9_fm[:, th * 512:(th + 1) * 512],
                                                            bank[0:9, :]),
                  incs=[("dve", 1)])
        setrd(("pb", th), ev)
    T_r9 = B.at("dve")

    for tc in range(OTC):
        tp = task("tensor", [("dve", T_r9)] + rd(("pc", 0)),
                  lambda E, tc=tc: E.transpose(pc[0][:, 0:9], r9_fm[:, tc * P:(tc + 1) * P],
                                               idn_f32[0:9, 0:9]),
                  incs=[("pe", 1)])
        ev = task("vector", [("pe", tp["pe"])],
                  lambda E, tc=tc: E.tensor_copy(r9_tm[:, tc, :], pc[0][:, 0:9]),
                  incs=[("dve", 1)])
        setrd(("pc", 0), ev)
    T_rtm = B.at("dve")

    def step(eng, prev, body):
        sem = "dve" if eng == "vector" else "act"
        return task(eng, prev, body, incs=[(sem, 1)])

    for tc in range(OTC):
        t = step("vector", [("dve", T_rtm)],
                 lambda E, tc=tc: E.reduce_max(mx1[:], r9_tm[:, tc, 0:8], axis=AX.X, negate=True))
        t = step("scalar", [("dve", t["dve"])],
                 lambda E, tc=tc: E.activation(e_sb[:], r9_tm[:, tc, 0:8], AF.Exp, bias=mx1[:, 0:1]))
        t = step("vector", [("act", t["act"])],
                 lambda E: E.reduce_sum(sden[:], e_sb[:], axis=AX.X))
        t = step("vector", [("dve", t["dve"])], lambda E: E.reciprocal(rs9[:], sden[:]))
        t = step("vector", [("dve", t["dve"])],
                 lambda E: E.tensor_scalar(pr_sb[:], e_sb[:], rs9[:, 0:1], None, ALU.mult))
        t = step("vector", [("dve", t["dve"])],
                 lambda E: E.reduce_max(mx1[:], pr_sb[:], axis=AX.X))
        t = step("vector", [("dve", t["dve"])],
                 lambda E: E.tensor_scalar(msk_sb[:], pr_sb[:], mx1[:, 0:1], None, ALU.is_equal))
        t = step("vector", [("dve", t["dve"])],
                 lambda E: E.scalar_tensor_tensor(msk_sb[:], msk_sb[:], float(-1e30),
                                                  pr_sb[:], ALU.mult, ALU.add))
        t = step("vector", [("dve", t["dve"])],
                 lambda E: E.reduce_max(m2v[:], msk_sb[:], axis=AX.X))
        t = step("vector", [("dve", t["dve"])],
                 lambda E: E.tensor_scalar(ge_sb[:], pr_sb[:], m2v[:, 0:1], None, ALU.is_ge))
        t = step("vector", [("dve", t["dve"])],
                 lambda E, tc=tc: E.tensor_tensor(cg_tm[:, tc, 0:8], ge_sb[:], pr_sb[:], ALU.mult))
    T_comb = B.at("dve")
    for tc in range(OTC):
        task("scalar", [("dve", T_comb)],
             lambda E, tc=tc: E.activation(cg_tm[:, tc, 8:9], r9_tm[:, tc, 8:9], AF.Sigmoid),
             incs=[("act", 1)])
    T_sig = B.at("act")
    for tc in range(OTC):
        tp = task("tensor", [("act", T_sig), ("dve", T_comb)] + rd(("pc", 1)),
                  lambda E, tc=tc: E.transpose(pc[1][0:9, 0:P], cg_tm[:, tc, :], idn_f32[:]),
                  incs=[("pe", 1)])
        ev = task("vector", [("pe", tp["pe"])],
                  lambda E, tc=tc: E.tensor_copy(cg_fm[:, tc * P:(tc + 1) * P], pc[1][0:9, 0:P]),
                  incs=[("dve", 1)])
        setrd(("pc", 1), ev)
    T_cg = B.at("dve")
    ckpt(5)

    u6 = [0]
    up = [0]
    T_out = {}
    for e in range(NEXP):
        row = e if e < 8 else 8
        T_cbc = None
        for th in range(2):
            tok = task("tensor", [("xdma", oh_tok["xdma"]), ("dve", T_cg)] + rd(("pd",)),
                       lambda E, row=row, th=th: E.matmul(
                           pd[:], oh9_sb[:, row * P:(row + 1) * P],
                           cg_fm[:, th * 512:(th + 1) * 512], start=True, stop=True),
                       incs=[("pe", 1)])
            ev = task("vector", [("pe", tok["pe"])],
                      lambda E, e=e, th=th: E.tensor_copy(cbc[e % 2][:, th * 512:(th + 1) * 512],
                                                          pd[:]),
                      incs=[("dve", 1)])
            setrd(("pd",), ev)
            T_cbc = ev["dve"]

        for f in range(FC):
            buf, wtok = grouped(d["fc12_s"][e], f * 2, 2, HID)
            lastok = None
            for th in range(2):
                u = u6[0]
                u6[0] += 1
                b1i, b2i = 2 * (u % 2), 2 * (u % 2) + 1
                b1, b2 = pb[b1i], pb[b2i]
                waits = [("wdma", wtok), ("dve", T_xn2done)] + rd(("pb", b1i)) + rd(("pb", b2i))

                def body(E, buf=buf, th=th, b1=b1, b2=b2):
                    ins = None
                    for kc in range(CC):
                        E.matmul(b1[:], wpair[buf][:, kc * P:(kc + 1) * P],
                                 xn2[:, kc, th * 512:(th + 1) * 512],
                                 start=(kc == 0), stop=(kc == CC - 1))
                    for kc in range(CC):
                        ins = E.matmul(b2[:], wpair[buf][:, HID + kc * P:HID + (kc + 1) * P],
                                       xn2[:, kc, th * 512:(th + 1) * 512],
                                       start=(kc == 0), stop=(kc == CC - 1))
                    return ins
                tok = task("tensor", waits, body, incs=[("pe", 1)])
                lastok = tok
                ta = task("scalar", [("pe", tok["pe"])],
                          lambda E, u=u, b1=b1: E.activation(silu_sb[u % 2][:], b1[:], AF.Silu),
                          incs=[("act", 1)])
                task("vector", [("act", ta["act"]), ("dve", T_cbc)],
                     lambda E, u=u, b2=b2: E.tensor_tensor(t1_sb[u % 2][:], b2[:],
                                                           silu_sb[u % 2][:], ALU.mult))
                ev = task("vector", [],
                          lambda E, e=e, f=f, th=th, u=u: E.tensor_tensor(
                              h_sb[:, f, th * 512:(th + 1) * 512], t1_sb[u % 2][:],
                              cbc[e % 2][:, th * 512:(th + 1) * 512], ALU.mult),
                          incs=[("dve", 1)])
                setrd(("pb", b1i), ev)
                setrd(("pb", b2i), ev)
            setrd(("wpair", buf), lastok)
        T_h = B.at("dve")

        for cb in range(4):
            gA = grouped(d["proj_s"][e, cb], 0, 8, 512)
            gB = grouped(d["proj_s"][e, cb], 8, 3, 512)
            lastok = None
            for tc in range(OTC):
                u = up[0]
                up[0] += 1
                bi = u % 2
                bank = pc[bi]
                waits = [("dve", T_h), ("wdma", gB[1])] + rd(("pc", bi))

                def body(E, tc=tc, bank=bank, gA=gA, gB=gB):
                    ins = None
                    for f in range(FC):
                        buf = gA[0] if f < 8 else gB[0]
                        ins = E.matmul(bank[:], h_sb[:, f, tc * P:(tc + 1) * P],
                                       wpair[buf][:, (f % 8) * 512:(f % 8 + 1) * 512],
                                       start=(f == 0), stop=(f == FC - 1))
                    return ins
                tok = task("tensor", waits, body, incs=[("pe", 1)])
                lastok = tok
                ev = task("vector", [("pe", tok["pe"])],
                          lambda E, tc=tc, cb=cb, bank=bank: E.tensor_tensor(
                              x1[:, tc, cb * 512:(cb + 1) * 512], bank[:],
                              x1[:, tc, cb * 512:(cb + 1) * 512], ALU.add),
                          incs=[("dve", 1)])
                setrd(("pc", bi), ev)
                if e == NEXP - 1 and cb == 3:
                    T_out[tc] = ev["dve"]
            setrd(("wpair", gA[0]), lastok)
            setrd(("wpair", gB[0]), lastok)

    for tc in range(OTC):
        task("sync", [("dve", T_out[tc])],
             lambda E, tc=tc: E.dma_start(out=d["out"][tc * P:(tc + 1) * P, :], in_=x1[:, tc, :]),
             incs=[("odma", 16)])

    stopped[0] = False
    if taps:
        tw = [(s, B.at(s)) for s in ("pe", "dve", "act")]
        for name in taps:
            t = tap_reg[name]
            dt_ = nc.dram_tensor("tap_" + name, list(t.shape), t.dtype,
                                 kind="ExternalOutput").ap()
            task("sync", tw, lambda E, t=t, dt_=dt_: E.dma_start(out=dt_[:], in_=t[:]),
                 incs=[("odma", 16)])
    task("sync", [("odma", B.at("odma"))], lambda E: E.nop())

    for s in ["xdma", "wdma", "odma", "pe", "dve", "act"]:
        nc.gpsimd.sem_clear(B.S[s])
    nc.all_engine_barrier()

    with nc.Block() as block:
        @block.sync
        def _(E):
            for f in B.ops["sync"]:
                f(E)

        @block.tensor
        def _(E):
            for f in B.ops["tensor"]:
                f(E)

        @block.vector
        def _(E):
            for f in B.ops["vector"]:
                f(E)

        @block.scalar
        def _(E):
            for f in B.ops["scalar"]:
                f(E)

        @block.gpsimd
        def _(E):
            for f in B.ops["gpsimd"]:
                f(E)

    stack.close()
    return nc


def kernel(**inputs):
    shared, per_core = _host_prep(inputs)
    nc = bass.Bass("TRN2", target_bir_lowering=False)
    build_program(nc)
    in_maps = []
    for c in range(N_CORES):
        m = dict(per_core[c])
        m.update(shared)
        in_maps.append(m)
    res = run_bass_kernel_spmd(nc, in_maps, core_ids=list(range(N_CORES)))
    out = np.concatenate([np.asarray(r["out"], dtype=np.float32) for r in res.results], axis=0)
    return np.ascontiguousarray(out.reshape(1, T_FULL, HID))


if __name__ == "__main__":
    import jax
    jax.config.update('jax_platforms', 'cpu')
    import reference as R
    inp = R.setup_inputs(0)
    inp = {k: np.asarray(v) for k, v in inp.items()}
    out = kernel(**inp)
    ref = np.asarray(R.reference(**inp))
    num = float(((out - ref) ** 2).mean())
    den = float((ref ** 2).mean())
    print(f"rvr: {num/den:.3e}")
